# version 17
# baseline (speedup 1.0000x reference)
import numpy as np
import ml_dtypes
from contextlib import ExitStack

import concourse.bass as bass
import concourse.mybir as mybir
from concourse.bass_utils import run_bass_kernel_spmd

F32 = mybir.dt.float32
BF16 = mybir.dt.bfloat16
AF = mybir.ActivationFunctionType
ALU = mybir.AluOpType
AX = mybir.AxisListType.X

N_CORES = 8
T = 2048
C = 1024
H = 16
DH = 64
CS = 512
NG = 4
NT = 16
NE = 8
HID = 1536
EPS = 1e-5

ENGINES = ("tensor", "vector", "scalar", "sync", "gpsimd")


class Rec:

    def __init__(self):
        self.ops = {e: [] for e in ENGINES}
        self.cnt = {k: 0 for k in (
            "t", "v", "s", "g",
            "cst", "x0", "x1",
            "wv", "wqk", "wo", "w1a", "w1b", "w3a", "w3b", "w2",
            "st0", "st1", "ac0", "ac1", "ac2", "ac3")}
        self.sems = {}
        self._waited = {}

    DMA_SEMS = ("cst", "x0", "x1", "wv", "wqk", "wo", "w1a", "w1b",
                "w3a", "w3b", "w2", "st0", "st1", "ac0", "ac1", "ac2", "ac3")

    def inc(self, sem, n=1):
        if sem in self.DMA_SEMS:
            n *= 16
        self.cnt[sem] += n
        return (sem, self.cnt[sem], n)

    OWN_SEM = {"tensor": "t", "vector": "v", "scalar": "s", "gpsimd": "g"}

    def emit(self, eng, method, *args, waits=(), inc=None, **kw):
        ws = []
        own = self.OWN_SEM.get(eng)
        for (s, v) in waits:
            if v <= 0:
                continue
            key = (eng, s)
            if s != own and self._waited.get(key, -1) >= v:
                continue
            self._waited[key] = v
            ws.append((s, v))
        ws = tuple(ws)
        incs = None if inc is None else (inc[0], inc[2])

        def f(E, method=method, args=args, kw=kw, ws=ws, incs=incs):
            for s, v in ws:
                E.wait_ge(self.sems[s], v)
            r = getattr(E, method)(*args, **kw)
            if incs is not None:
                r.then_inc(self.sems[incs[0]], incs[1])
        self.ops[eng].append(f)

    def wait(self, eng, waits):
        return waits


def build_program(n_phases=7, taps=()):
    nc = bass.Bass("TRN2", target_bir_lowering=False)

    d_x = nc.dram_tensor("x", [T, C], F32, kind="ExternalInput").ap()
    d_wqkvT = nc.dram_tensor("wqkvT", [C, 3 * C], F32, kind="ExternalInput").ap()
    d_woutT = nc.dram_tensor("woutT", [C, C], F32, kind="ExternalInput").ap()
    d_bout = nc.dram_tensor("bout", [1, C], F32, kind="ExternalInput").ap()
    d_wl1T = nc.dram_tensor("wl1T", [1, 64], F32, kind="ExternalInput").ap()
    d_bl1 = nc.dram_tensor("bl1", [64, 1], F32, kind="ExternalInput").ap()
    d_wl2T = nc.dram_tensor("wl2T", [64, 16], F32, kind="ExternalInput").ap()
    d_bl2 = nc.dram_tensor("bl2", [16, 1], F32, kind="ExternalInput").ap()
    d_wr = nc.dram_tensor("wr", [C, NE], F32, kind="ExternalInput").ap()
    d_w1 = nc.dram_tensor("w1", [NE, C, HID], F32, kind="ExternalInput").ap()
    d_w2 = nc.dram_tensor("w2", [NE, HID, C], F32, kind="ExternalInput").ap()
    d_w3 = nc.dram_tensor("w3", [NE, C, HID], F32, kind="ExternalInput").ap()
    d_logpT = nc.dram_tensor("logpT", [1, T], F32, kind="ExternalInput").ap()
    d_ident = nc.dram_tensor("ident", [128, 128], F32, kind="ExternalInput").ap()
    d_out = nc.dram_tensor("out", [T, C], F32, kind="ExternalOutput").ap()

    tap_outs = {}

    with ExitStack() as ctx:
        e = ctx.enter_context

        ARENA = 115200
        arena = nc.alloc_sbuf_tensor("arena", [128, ARENA], mybir.dt.uint8)
        abase = nc.lookup_mloc(arena).addr

        def place(name, shape, dtype, off):
            nbytes = int(np.prod(shape[1:])) * mybir.dt.size(dtype)
            assert off + nbytes <= ARENA, (name, off, nbytes)
            return nc.alloc_sbuf_tensor_at(name, shape, dtype, offset=abase + off)

        k_fm = place("k_fm", [128, 16384], BF16, 0)
        logp_sb = place("logp_sb", [1, T], F32, 0)
        h1_sb = place("h1_sb", [64, T], F32, 8192)
        lam_fm = place("lam_fm", [16, T], F32, 16384)
        woutT_b = place("woutT_b", [128, 8192], BF16, 98816)
        accA = place("accA", [128, 2048], F32, 16384)
        x_slot = place("x_slot", [128, 2048], F32, 24576)
        a_fm = place("a_fm", [128, 6144], BF16, 0)
        v_sc = place("v_sc", [128, 16640], BF16, 32768)
        xn_f32 = place("xn_f32", [128, 2048], F32, 32768)
        xnT_f32 = place("xnT_f32", [128, 2048], F32, 40960)
        w2buf = place("w2buf", [128, 6144], BF16, 32768)
        w1q = place("w1q", [128, 6144], BF16, 45056)
        w3q = place("w3q", [128, 6144], BF16, 57344)
        wqkvT_b = place("wqkvT_b", [128, 16384], BF16, 66048)
        accB = place("accB", [128, 4096], F32, 69632)
        silu_t = place("silu_t", [128, 1024], F32, 86016)
        accC = place("accC", [128, 2048], F32, 90112)
        exp_sb = place("exp_sb", [128, 8192], BF16, 69632)
        wvT_b = place("wvT_b", [128, 8192], BF16, 98816)

        xn_fm = e(nc.sbuf_tensor("xn_fm", [128, 16384], BF16))
        q_fm = e(nc.sbuf_tensor("q_fm", [128, 16384], BF16))
        lamT = e(nc.sbuf_tensor("lamT", [128, 256], F32))
        ident_sb = e(nc.sbuf_tensor("ident_sb", [128, 128], F32))
        ones_sb = e(nc.sbuf_tensor("ones_sb", [128, 128], F32))
        ones_bf = e(nc.sbuf_tensor("ones_bf", [128, 64], BF16))
        wr_sb = e(nc.sbuf_tensor("wr_sb", [128, 64], F32))
        bout_r = e(nc.sbuf_tensor("bout_r", [1, C], F32))
        boutB = e(nc.sbuf_tensor("boutB", [128, C], F32))
        comb = e(nc.sbuf_tensor("comb", [128, 128], F32))
        logits = e(nc.sbuf_tensor("logits", [128, 128], F32))
        xsq = e(nc.sbuf_tensor("xsq", [128, C], F32))
        stat = e(nc.sbuf_tensor("stat", [128, 32], F32))
        cscr = e(nc.sbuf_tensor("cscr", [128, 64], F32))
        rcp_sb = e(nc.sbuf_tensor("rcp_sb", [128, 2048], BF16))
        ln_st = e(nc.sbuf_tensor("ln_st", [1, 1024], F32))
        rB_sb = e(nc.sbuf_tensor("rB_sb", [64, 1024], F32))
        wl1T_sb = e(nc.sbuf_tensor("wl1T_sb", [1, 64], F32))
        bl1_sb = e(nc.sbuf_tensor("bl1_sb", [64, 1], F32))
        wl2T_sb = e(nc.sbuf_tensor("wl2T_sb", [64, 16], F32))
        bl2_sb = e(nc.sbuf_tensor("bl2_sb", [16, 1], F32))

        pb = [e(nc.psum_tensor(f"pb{i}", [128, 512], F32)) for i in range(8)]

        R = Rec()
        sem_handles = {}
        for name in R.cnt:
            sem_handles[name] = e(nc.semaphore(f"sem_{name}"))
        R.sems = sem_handles

        v_view = v_sc[:].rearrange("p (ti h d) -> p ti h d", ti=16, h=16)

        R.emit("gpsimd", "memset", ones_sb[:], 1.0, inc=R.inc("g"))
        R.emit("gpsimd", "memset", ones_bf[:], 1.0, inc=R.inc("g"))
        g_ones = R.cnt["g"]
        R.emit("gpsimd", "dma_start",
               out=wvT_b[:].rearrange("p (K f) -> p K f", K=8),
               in_=d_wqkvT[:, 2048:3072].rearrange("(K p) f -> p K f", p=128),
               inc=R.inc("wv"))
        wd_wave2 = R.cnt["wv"]
        R.emit("gpsimd", "dma_start",
               out=wqkvT_b[:].rearrange("p (K f) -> p K f", K=8),
               in_=d_wqkvT[:, 0:2048].rearrange("(K p) f -> p K f", p=128),
               inc=R.inc("wqk"))
        wd_wave1 = R.cnt["wqk"]

        for ap_out, ap_in in (
            (logp_sb[:], d_logpT), (wl1T_sb[:], d_wl1T), (bl1_sb[:], d_bl1),
            (wl2T_sb[:], d_wl2T), (bl2_sb[:], d_bl2), (ident_sb[:], d_ident),
            (wr_sb[:].rearrange("p (K n) -> p K n", K=8),
             d_wr.rearrange("(K p) n -> p K n", p=128)),
            (bout_r[:], d_bout),
        ):
            R.emit("sync", "dma_start", out=ap_out, in_=ap_in, inc=R.inc("cst"))
        xd_consts = R.cnt["cst"]

        lam_sig_tick = []
        for i in range(4):
            R.emit("tensor", "matmul", pb[i % 2][0:64, :], wl1T_sb[0:1, :],
                   logp_sb[0:1, i * 512:(i + 1) * 512], start=True, stop=True,
                   waits=[("cst", xd_consts), ("s", R.cnt["s"])],
                   inc=R.inc("t"))
            R.emit("scalar", "activation", h1_sb[0:64, i * 512:(i + 1) * 512],
                   pb[i % 2][0:64, :], AF.Relu, bias=bl1_sb[0:64, 0:1],
                   waits=[("t", R.cnt["t"])], inc=R.inc("s"))
        for i in range(4):
            R.emit("tensor", "matmul", pb[2 + i % 2][0:16, :], wl2T_sb[0:64, :],
                   h1_sb[0:64, i * 512:(i + 1) * 512], start=True, stop=True,
                   waits=[("s", R.cnt["s"])], inc=R.inc("t"))
            R.emit("scalar", "activation", lam_fm[0:16, i * 512:(i + 1) * 512],
                   pb[2 + i % 2][0:16, :], AF.Sigmoid, bias=bl2_sb[0:16, 0:1],
                   waits=[("t", R.cnt["t"])], inc=R.inc("s"))
            lam_sig_tick.append(R.cnt["s"])
        for i in range(16):
            R.emit("tensor", "matmul", pb[4][:, i * 16:(i + 1) * 16],
                   lam_fm[0:16, i * 128:(i + 1) * 128], ident_sb[0:16, 0:16],
                   is_transpose=True, start=True, stop=True,
                   waits=[("s", lam_sig_tick[i // 4])],
                   inc=(R.inc("t") if i == 15 else None))
        t_lamT = R.cnt["t"]
        R.emit("vector", "tensor_scalar", lamT[:, 0:256], pb[4][:, 0:256],
               0.125, None, op0=ALU.mult, waits=[("t", t_lamT)], inc=R.inc("v"))
        v_lamT = R.cnt["v"]

        if n_phases < 2:
            return _finish(nc, R, ctx, taps, tap_outs, locals())

        xn_fm_v = xn_fm[:].rearrange("p (c m) -> p c m", c=8)
        xslot_free = [[("v", 0), ("s", 0)], [("v", 0), ("s", 0)]]
        xsq_free = ("v", 0)
        tp_bank_free = [[("v", 0), ("s", 0)], [("v", 0), ("s", 0)]]
        gate_bank_free = [("v", 0), ("v", 0)]
        xnf_free = [("t", 0), ("t", 0)]
        xnT_free = [("t", 0), ("t", 0)]
        tp_tick = {}
        evac_tick = {}
        sv_bank_free = 0
        logits_tick = {}
        xn_tick = {}
        xn_done_tick = []
        for j in range(NT + 2):
            sl = j % 2
            if j < NT:
                xsem = "x0" if sl == 0 else "x1"
                R.emit("sync", "dma_start", out=x_slot[:, sl * 1024:(sl + 1) * 1024],
                       in_=d_x[j * 128:(j + 1) * 128, :],
                       waits=xslot_free[sl], inc=R.inc(xsem))
                xw = [(xsem, R.cnt[xsem])]
                xsl = x_slot[:, sl * 1024:(sl + 1) * 1024]
                sb = sl * 16
                R.emit("vector", "reduce_sum", stat[:, sb:sb + 1], xsl, AX,
                       waits=xw + [("s", R.cnt["s"])])
                R.emit("scalar", "activation", xsq[:], xsl, AF.Square,
                       waits=xw + [xsq_free], inc=R.inc("s"))
                R.emit("vector", "reduce_sum", stat[:, sb + 1:sb + 2], xsq[:], AX,
                       waits=[("s", R.cnt["s"])], inc=R.inc("v"))
                xsq_free = ("v", R.cnt["v"])
                v_r2 = R.cnt["v"]
                R.emit("vector", "tensor_scalar", stat[:, sb + 5:sb + 6],
                       stat[:, sb:sb + 1], 1.0 / C, None, op0=ALU.mult,
                       waits=[("v", v_r2)], inc=R.inc("v"))
                R.emit("vector", "tensor_tensor", stat[:, sb + 3:sb + 4],
                       stat[:, sb + 5:sb + 6], stat[:, sb + 5:sb + 6],
                       op=ALU.mult, waits=[("v", R.cnt["v"])])
                R.emit("vector", "tensor_scalar", stat[:, sb + 2:sb + 3],
                       stat[:, sb + 1:sb + 2], 1.0 / C, None, op0=ALU.mult,
                       waits=[("v", v_r2)], inc=R.inc("v"))
                R.emit("vector", "tensor_tensor", stat[:, sb + 4:sb + 5],
                       stat[:, sb + 2:sb + 3], stat[:, sb + 3:sb + 4],
                       op=ALU.subtract, waits=[("v", R.cnt["v"])], inc=R.inc("v"))
                R.emit("vector", "tensor_scalar", stat[:, sb + 8:sb + 9],
                       stat[:, sb + 4:sb + 5], EPS, None, op0=ALU.add,
                       waits=[("v", R.cnt["v"])], inc=R.inc("v"))
                v_var = R.cnt["v"]
                R.emit("scalar", "activation", stat[:, sb + 6:sb + 7],
                       stat[:, sb + 8:sb + 9], AF.Sqrt, waits=[("v", v_var)],
                       inc=R.inc("s"))
                R.emit("vector", "reciprocal", stat[:, sb + 7:sb + 8],
                       stat[:, sb + 6:sb + 7], waits=[("s", R.cnt["s"])],
                       inc=R.inc("v"))
                R.emit("vector", "tensor_scalar",
                       xn_f32[:, sl * 1024:(sl + 1) * 1024], xsl,
                       stat[:, sb + 5:sb + 6], stat[:, sb + 7:sb + 8],
                       op0=ALU.subtract, op1=ALU.mult,
                       waits=[("v", R.cnt["v"]), xnf_free[sl]], inc=R.inc("v"))
                xn_tick[j] = R.cnt["v"]
                xslot_free[sl] = [("v", xn_tick[j]), ("s", R.cnt["s"])]
            i = j - 1
            if 0 <= i < NT:
                isl = i % 2
                bk = isl * 3
                for c in range(8):
                    R.emit("tensor", "matmul",
                           pb[bk + c // 4][:, (c % 4) * 128:(c % 4 + 1) * 128],
                           xn_f32[:, isl * 1024 + c * 128: isl * 1024 + (c + 1) * 128],
                           ident_sb[:], is_transpose=True, start=True, stop=True,
                           waits=[("v", xn_tick[i])] + tp_bank_free[isl],
                           inc=(R.inc("t") if c == 7 else None))
                tp_tick[i] = R.cnt["t"]
                xnf_free[isl] = ("t", tp_tick[i])
            i2 = j - 2
            if 0 <= i2 < NT:
                i2sl = i2 % 2
                bk2 = i2sl * 3 + 2
                for K in range(8):
                    R.emit("tensor", "matmul", pb[bk2][:, 0:8],
                           xnT_f32[:, i2sl * 1024 + K * 128: i2sl * 1024 + (K + 1) * 128],
                           wr_sb[:, K * 8:(K + 1) * 8],
                           start=(K == 0), stop=(K == 7),
                           waits=[("s", evac_tick[i2][1]),
                                  ("v", evac_tick[i2][0]), gate_bank_free[i2sl]],
                           inc=(R.inc("t") if K == 7 else None))
                xnT_free[i2sl] = ("t", R.cnt["t"])
                gate_tick = R.cnt["t"]
            if 0 <= i < NT:
                isl = i % 2
                bk = isl * 3
                xnT_v = xnT_f32[:, isl * 1024:(isl + 1) * 1024].rearrange(
                    "p (c m) -> p c m", c=8)
                psv0 = pb[bk][:].rearrange("p (c m) -> p c m", c=4)
                psv1 = pb[bk + 1][:].rearrange("p (c m) -> p c m", c=4)
                R.emit("vector", "tensor_copy",
                       out=xn_fm_v[:, 0:4, i * 128:(i + 1) * 128],
                       in_=psv0, waits=[("t", tp_tick[i])])
                R.emit("vector", "tensor_copy", out=xnT_v[:, 0:4, :],
                       in_=psv0, waits=[xnT_free[isl]], inc=R.inc("v"))
                R.emit("scalar", "activation",
                       out=xn_fm_v[:, 4:8, i * 128:(i + 1) * 128],
                       in_=psv1, func=AF.Copy,
                       waits=[("t", tp_tick[i]), xnT_free[isl]])
                R.emit("scalar", "activation", out=xnT_v[:, 4:8, :],
                       in_=psv1, func=AF.Copy, inc=R.inc("s"))
                evac_tick[i] = (R.cnt["v"], R.cnt["s"])
                tp_bank_free[isl] = [("v", R.cnt["v"]), ("s", R.cnt["s"])]
                xn_done_tick.append(R.cnt["v"])
            if 0 <= i2 < NT:
                R.emit("vector", "tensor_copy", out=logits[:, i2 * 8:(i2 + 1) * 8],
                       in_=pb[i2 % 2 * 3 + 2][:, 0:8], waits=[("t", gate_tick)],
                       inc=R.inc("v"))
                gate_bank_free[i2 % 2] = ("v", R.cnt["v"])
                logits_tick[i2] = R.cnt["v"]
            tv = j - 2
            if 8 <= tv < NT:
                for K in range(8):
                    for seg in range(2):
                        R.emit("tensor", "matmul", pb[6 + seg][:],
                               xn_fm[:, K * 2048 + tv * 128: K * 2048 + (tv + 1) * 128],
                               wvT_b[:, K * 1024 + seg * 512: K * 1024 + (seg + 1) * 512],
                               start=(K == 0), stop=(K == 7),
                               waits=[("wv", wd_wave2), ("v", evac_tick[tv][0]),
                                      ("s", max(evac_tick[tv][1], sv_bank_free))],
                               inc=(R.inc("t") if (K == 7 and seg == 1) else None))
                t_v2 = R.cnt["t"]
                for seg in range(2):
                    R.emit("scalar", "activation",
                           v_view[:, tv, seg * 8:(seg + 1) * 8, 0:64]
                           if False else
                           v_view[:, tv, seg * 8:(seg + 1) * 8, 0:64],
                           pb[6 + seg][:].rearrange("p (h d) -> p h d", h=8),
                           AF.Copy, waits=[("t", t_v2)],
                           inc=(R.inc("s") if seg == 1 else None))
                sv_bank_free = R.cnt["s"]
        v_xn_all = R.cnt["v"]
        s_vP2_all = R.cnt["s"]
        xn_all_done = [("v", R.cnt["v"]), ("s", R.cnt["s"])]
        R.emit("gpsimd", "memset", v_view[:, :, :, 64:65], 1.0,
               waits=xn_all_done, inc=R.inc("g"))
        g_ones = R.cnt["g"]


        if n_phases < 3:
            return _finish(nc, R, ctx, taps, tap_outs, locals())

        def emit_comb(t):
                L = logits[:, t * 8:(t + 1) * 8]
                m1, nm1, m2, em, den, rden = (cscr[:, 32 + j:33 + j] for j in range(6))
                ge1 = cscr[:, 0:8]
                mskt = cscr[:, 8:16]
                e1 = cscr[:, 16:24]
                mask2 = cscr[:, 24:32]
                R.emit("vector", "reduce_max", m1, L, AX,
                       waits=[("s", R.cnt["s"]), ("v", logits_tick[t])],
                       inc=R.inc("v"))
                R.emit("vector", "reduce_max", nm1, L, AX, negate=True)
                R.emit("vector", "tensor_scalar", ge1, L, m1, None, op0=ALU.is_ge,
                       waits=[("v", R.cnt["v"])], inc=R.inc("v"))
                R.emit("vector", "tensor_scalar", ge1, ge1, -1e30, None,
                       op0=ALU.mult, waits=[("v", R.cnt["v"])], inc=R.inc("v"))
                R.emit("vector", "tensor_tensor", mskt, L, ge1, op=ALU.add,
                       waits=[("v", R.cnt["v"])], inc=R.inc("v"))
                R.emit("vector", "reduce_max", m2, mskt, AX,
                       waits=[("v", R.cnt["v"])], inc=R.inc("v"))
                v_m2 = R.cnt["v"]
                R.emit("scalar", "activation", e1, L, AF.Exp, bias=nm1,
                       waits=[("v", v_m2)])
                R.emit("scalar", "activation", em, m2, AF.Exp, bias=nm1,
                       inc=R.inc("s"))
                s_exp = R.cnt["s"]
                R.emit("vector", "tensor_scalar", mask2, L, m2, None, op0=ALU.is_ge,
                       waits=[("s", s_exp)])
                R.emit("vector", "tensor_scalar", den, em, 1.0, None, op0=ALU.add,
                       inc=R.inc("v"))
                R.emit("vector", "reciprocal", rden, den,
                       waits=[("v", R.cnt["v"])])
                R.emit("vector", "tensor_tensor", e1, e1, mask2, op=ALU.mult,
                       waits=[("v", R.cnt["v"])], inc=R.inc("v"))
                R.emit("vector", "tensor_scalar", comb[:, t * 8:(t + 1) * 8], e1,
                       rden, None, op0=ALU.mult, waits=[("v", R.cnt["v"])],
                       inc=R.inc("v"))
        qk_bank_free = [("v", 0), ("v", 0)]
        for fc in range(16):
            banks = [0, 1, 2, 3] if fc % 2 == 0 else [4, 5, 6, 7]
            qk_extra = [("s", s_vP2_all)] if fc % 2 == 1 else []
            for K in range(8):
                for tt in range(4):
                    R.emit("tensor", "matmul", pb[banks[tt]][:],
                           wqkvT_b[:, K * 2048 + fc * 128: K * 2048 + (fc + 1) * 128],
                           xn_fm[:, K * 2048 + tt * 512: K * 2048 + (tt + 1) * 512],
                           start=(K == 0), stop=(K == 7),
                           waits=[("wqk", wd_wave1), qk_bank_free[fc % 2]]
                                 + xn_all_done + qk_extra,
                           inc=(R.inc("t") if (K == 7 and tt == 3) else None))
            t_fc = R.cnt["t"]
            dst = q_fm if fc < 8 else k_fm
            cc = fc % 8
            for tt in range(4):
                R.emit("vector", "tensor_copy",
                       out=dst[:, cc * 2048 + tt * 512: cc * 2048 + (tt + 1) * 512],
                       in_=pb[banks[tt]][:], waits=[("t", t_fc)],
                       inc=(R.inc("v") if tt == 3 else None))
            qk_bank_free[fc % 2] = ("v", R.cnt["v"])
            emit_comb(fc)
        v_qk_all = R.cnt["v"]
        v_comb = R.cnt["v"]

        v_bank_free = [("v", v_qk_all), ("v", v_qk_all)]
        for t in range(8):
            b0 = 0 if t % 2 == 0 else 2
            for K in range(8):
                for seg in range(2):
                    R.emit("tensor", "matmul", pb[b0 + seg][:],
                           xn_fm[:, K * 2048 + t * 128: K * 2048 + (t + 1) * 128],
                           wvT_b[:, K * 1024 + seg * 512: K * 1024 + (seg + 1) * 512],
                           start=(K == 0), stop=(K == 7),
                           waits=[("wv", wd_wave2), v_bank_free[t % 2],
                                  ("g", g_ones)],
                           inc=(R.inc("t") if (K == 7 and seg == 1) else None))
            t_v = R.cnt["t"]
            for seg in range(2):
                R.emit("vector", "tensor_copy",
                       out=v_view[:, t, seg * 8:(seg + 1) * 8, 0:64],
                       in_=pb[b0 + seg][:].rearrange("p (h d) -> p h d", h=8),
                       waits=[("t", t_v)],
                       inc=(R.inc("v") if seg == 1 else None))
            v_bank_free[t % 2] = ("v", R.cnt["v"])
        v_v_all = R.cnt["v"]
        t_v_mm_all = R.cnt["t"]
        R.emit("gpsimd", "dma_start",
               out=woutT_b[:].rearrange("p (K f) -> p K f", K=8),
               in_=d_woutT.rearrange("(K p) f -> p K f", p=128),
               waits=[("t", t_v_mm_all)], inc=R.inc("wo"))
        wd_woutT = R.cnt["wo"]

        for seg in range(2):
            R.emit("tensor", "matmul", pb[4 + seg][:],
                   ones_sb[0:1, 0:128], bout_r[0:1, seg * 512:(seg + 1) * 512],
                   start=True, stop=True,
                   waits=[("cst", xd_consts), ("v", v_v_all), ("g", g_ones)],
                   inc=(R.inc("t") if seg == 1 else None))
        R.emit("vector", "tensor_copy", out=boutB[:, 0:512], in_=pb[4][:],
               waits=[("t", R.cnt["t"])])
        R.emit("vector", "tensor_copy", out=boutB[:, 512:1024], in_=pb[5][:],
               inc=R.inc("v"))
        v_boutB = R.cnt["v"]

        if n_phases < 4:
            return _finish(nc, R, ctx, taps, tap_outs, locals())

        NI = NG * H

        def _gh(i):
            return i // H, i % H

        sT_bank_free = [("s", 0)] * 4
        av_slot_free = [("v", v_boutB), ("v", v_boutB), ("v", v_v_all)]
        n_bank_free = ("s", 0)
        rcp_free = [("t", 0), ("t", 0)]
        rb_free = ("v", 0)
        exp_free = [("t", 0), ("t", 0)]
        exp_ticks = {}
        av_ticks = {}
        rcp_ticks = {}
        bc_ticks = {}
        rb_ticks = {}
        t_sT_all = None

        for j in range(NI + 2):
            if j < NI:
                g, h = _gh(j)
                hc, hr = h // 2, (h % 2) * 64
                for kc in range(4):
                    R.emit("tensor", "matmul", pb[kc][:],
                           k_fm[hr:hr + 64,
                                hc * 2048 + g * 512 + kc * 128: hc * 2048 + g * 512 + (kc + 1) * 128],
                           q_fm[hr:hr + 64, hc * 2048 + g * 512: hc * 2048 + (g + 1) * 512],
                           start=True, stop=True,
                           waits=[sT_bank_free[kc], ("v", v_v_all)],
                           inc=R.inc("t"))
                    exp_ticks.setdefault(j, []).append(None)
                    exp_ticks[j][kc] = ("t_sT", R.cnt["t"])
                if j == NI - 1:
                    t_sT_all = R.cnt["t"]
            if j < NI:
                g, h = _gh(j)
                par = j % 2
                for kc in range(4):
                    R.emit("scalar", "activation",
                           exp_sb[:, par * 2048 + kc * 512: par * 2048 + (kc + 1) * 512],
                           pb[kc][:], AF.Exp,
                           scale=lamT[:, (g * 4 + kc) * 16 + h: (g * 4 + kc) * 16 + h + 1],
                           waits=[("t", exp_ticks[j][kc][1]), ("v", v_lamT),
                                  exp_free[par]],
                           inc=R.inc("s"))
                    sT_bank_free[kc] = ("s", R.cnt["s"])
                    exp_ticks[j][kc] = ("s", R.cnt["s"])
            i = j - 1
            if 0 <= i < NI:
                g, h = _gh(i)
                par = i % 2
                bk = 4 + i % 3
                for kc in range(4):
                    R.emit("tensor", "matmul", pb[bk][0:65, :],
                           v_view[:, g * 4 + kc, h, :],
                           exp_sb[:, par * 2048 + kc * 512: par * 2048 + (kc + 1) * 512],
                           start=(kc == 0), stop=(kc == 3),
                           waits=[exp_ticks[i][kc], ("v", v_v_all),
                                  ("s", s_vP2_all), ("g", g_ones)] +
                                 ([av_slot_free[i % 3]] if kc == 0 else []),
                           inc=(R.inc("t") if kc == 3 else None))
                av_ticks[i] = R.cnt["t"]
                exp_free[par] = ("t", av_ticks[i])
            if 0 <= i < NI:
                R.emit("vector", "reciprocal", rcp_sb[64:65, (i % 2) * 512:(i % 2) * 512 + 512],
                       pb[4 + i % 3][64:65, :],
                       waits=[("t", av_ticks[i]), rcp_free[i % 2]],
                       inc=R.inc("v"))
                rcp_ticks[i] = R.cnt["v"]
            i2 = j - 2
            if 0 <= i2 < NI:
                R.emit("tensor", "matmul", pb[7][0:64, :],
                       ones_bf[64:65, 0:64],
                       rcp_sb[64:65, (i2 % 2) * 512:(i2 % 2) * 512 + 512],
                       start=True, stop=True,
                       waits=[("v", rcp_ticks[i2]), n_bank_free, ("g", g_ones)],
                       inc=R.inc("t"))
                bc_ticks[i2] = R.cnt["t"]
                rcp_free[i2 % 2] = ("t", bc_ticks[i2])
            if 0 <= i2 < NI:
                R.emit("scalar", "activation", rB_sb[0:64, 0:512], pb[7][0:64, :],
                       AF.Copy, waits=[("t", bc_ticks[i2]), rb_free],
                       inc=R.inc("s"))
                rb_ticks[i2] = R.cnt["s"]
                n_bank_free = ("s", rb_ticks[i2])
            if 0 <= i2 < NI:
                g, h = _gh(i2)
                hc, hr = h // 2, (h % 2) * 64
                R.emit("vector", "tensor_tensor",
                       q_fm[hr:hr + 64, hc * 2048 + g * 512: hc * 2048 + (g + 1) * 512],
                       rB_sb[0:64, 0:512], pb[4 + i2 % 3][0:64, :], op=ALU.mult,
                       waits=[("s", rb_ticks[i2]), ("v", v_qk_all)],
                       inc=R.inc("v"))
                av_slot_free[i2 % 3] = ("v", R.cnt["v"])
                rb_free = ("v", R.cnt["v"])
        v_ao_all = R.cnt["v"]
        t_av_all = R.cnt["t"]

        if n_phases < 6:
            return _finish(nc, R, ctx, taps, tap_outs, locals())

        def acc_ap(tl):
            if tl < 2:
                return accA[:, tl * 1024:(tl + 1) * 1024]
            if tl < 6:
                return accB[:, (tl - 2) * 1024:(tl - 1) * 1024]
            return accC[:, (tl - 6) * 1024:(tl - 5) * 1024]

        def w1q_dma(ei, Q, par):
            R.emit("gpsimd", "dma_start",
                   out=w1q[:, par * 3072:(par + 1) * 3072].rearrange("p (K h) -> p K h", K=8),
                   in_=d_w1[ei, :, Q * 384:(Q + 1) * 384].rearrange("(K p) h -> p K h", p=128),
                   waits=[w1q_free[par]], inc=R.inc("w1a" if par == 0 else "w1b"))
            w1q_tick[par] = R.cnt["w1a" if par == 0 else "w1b"]

        def w3q_dma(ei, Q, par):
            R.emit("gpsimd", "dma_start",
                   out=w3q[:, par * 3072:(par + 1) * 3072].rearrange("p (K h) -> p K h", K=8),
                   in_=d_w3[ei, :, Q * 384:(Q + 1) * 384].rearrange("(K p) h -> p K h", p=128),
                   waits=[w3q_free[par]], inc=R.inc("w3a" if par == 0 else "w3b"))
            w3q_tick[par] = R.cnt["w3a" if par == 0 else "w3b"]

        def w2_dma(ei, hf):
            R.emit("gpsimd", "dma_start",
                   out=w2buf[:].rearrange("p (J c) -> p J c", J=6),
                   in_=d_w2[ei, hf * 768:(hf + 1) * 768, :].rearrange("(J p) c -> p J c", p=128),
                   waits=[w2_free[0]], inc=R.inc("w2"))
            w2_tick[0] = R.cnt["w2"]

        w1q_free = [("t", t_av_all), ("t", t_av_all)]
        w3q_free = [("t", t_av_all), ("t", t_av_all)]
        w2_free = [("t", t_av_all)]
        w1q_tick = [0, 0]
        w3q_tick = [0, 0]
        w2_tick = [0]
        passes = [(th, ei, hf) for th in range(2) for ei in range(NE)
                  for hf in range(2)]
        w2_dma(0, 0)
        w1q_dma(0, 0, 0)
        w3q_dma(0, 0, 0)
        w1q_dma(0, 1, 1)
        w3q_dma(0, 1, 1)

        wout_bank_free = [[("v", v_boutB)], [("v", v_boutB)]]
        xslot_free2 = [[("t", t_sT_all)], [("t", t_sT_all)]]
        acc_free = [[] for _ in range(8)]
        hg_bank_free = [("v", 0)] * 8
        y_bank_free = {2: ("v", 0), 3: ("v", 0), 6: ("v", 0), 7: ("v", 0)}
        silu_free = [("v", 0), ("v", 0)]
        acc_last = [0] * 8
        n_stores = 0

        for th in range(2):
            if th == 1:
                wout_bank_free = [[("v", R.cnt["v"]), ("s", R.cnt["s"])]] * 2
            for tl in range(8):
                t = th * 8 + tl
                sl = tl % 2
                b0 = sl * 2
                xsem = "x0" if sl == 0 else "x1"
                R.emit("sync", "dma_start", out=x_slot[:, sl * 1024:(sl + 1) * 1024],
                       in_=d_x[t * 128:(t + 1) * 128, :],
                       waits=xslot_free2[sl], inc=R.inc(xsem))
                xd_x2 = R.cnt[xsem]
                for seg in range(2):
                    for K in range(8):
                        R.emit("tensor", "matmul", pb[b0 + seg][:],
                               q_fm[:, K * 2048 + t * 128: K * 2048 + (t + 1) * 128],
                               woutT_b[:, K * 1024 + seg * 512: K * 1024 + (seg + 1) * 512],
                               start=(K == 0), stop=(K == 7),
                               waits=[("wo", wd_woutT), ("v", v_ao_all)]
                                     + wout_bank_free[sl],
                               inc=(R.inc("t") if (K == 7 and seg == 1) else None))
                t_wout = R.cnt["t"]
                acl = acc_ap(tl)
                for seg in range(2):
                    R.emit("vector", "tensor_tensor",
                           acl[:, seg * 512:(seg + 1) * 512],
                           boutB[:, seg * 512:(seg + 1) * 512],
                           pb[b0 + seg][:], op=ALU.add,
                           waits=[("t", t_wout), ("v", v_boutB)] + acc_free[tl],
                           inc=R.inc("v"))
                wout_bank_free[sl] = [("v", R.cnt["v"])]
                R.emit("vector", "tensor_tensor", acl, acl,
                       x_slot[:, sl * 1024:(sl + 1) * 1024], op=ALU.add,
                       waits=[(xsem, xd_x2), ("v", R.cnt["v"])], inc=R.inc("v"))
                acc_last[tl] = R.cnt["v"]
                xslot_free2[sl] = [("v", R.cnt["v"])]
            hg_bank_free = [("v", R.cnt["v"])] * 8
            for b in (2, 3, 6, 7):
                y_bank_free[b] = ("v", R.cnt["v"])
            for ei in range(NE):
                for hf in range(2):
                    w2_now = w2_tick[0]
                    v_a_tt = [0, 0]
                    for tt in range(2):
                        for ql in range(2):
                            Q = hf * 2 + ql
                            par = Q % 2
                            w1_now, w3_now = w1q_tick[par], w3q_tick[par]
                            w1sem = "w1a" if par == 0 else "w1b"
                            w3sem = "w3a" if par == 0 else "w3b"
                            for jj in range(3):
                                jl = ql * 3 + jj
                                bk = jl % 2
                                for K in range(8):
                                    R.emit("tensor", "matmul", pb[bk][:],
                                           w1q[:, par * 3072 + K * 384 + jj * 128:
                                               par * 3072 + K * 384 + (jj + 1) * 128],
                                           xn_fm[:, K * 2048 + th * 1024 + tt * 512:
                                                 K * 2048 + th * 1024 + (tt + 1) * 512],
                                           start=(K == 0), stop=(K == 7),
                                           waits=[(w1sem, w1_now), hg_bank_free[bk]],
                                           inc=(R.inc("t") if K == 7 else None))
                                t_h = R.cnt["t"]
                                for K in range(8):
                                    R.emit("tensor", "matmul", pb[4 + bk][:],
                                           w3q[:, par * 3072 + K * 384 + jj * 128:
                                               par * 3072 + K * 384 + (jj + 1) * 128],
                                           xn_fm[:, K * 2048 + th * 1024 + tt * 512:
                                                 K * 2048 + th * 1024 + (tt + 1) * 512],
                                           start=(K == 0), stop=(K == 7),
                                           waits=[(w3sem, w3_now), hg_bank_free[4 + bk]],
                                           inc=(R.inc("t") if K == 7 else None))
                                t_g = R.cnt["t"]
                                R.emit("scalar", "activation",
                                       silu_t[:, bk * 512:(bk + 1) * 512], pb[bk][:],
                                       AF.Silu, waits=[("t", t_h), silu_free[bk]],
                                       inc=R.inc("s"))
                                s_silu = R.cnt["s"]
                                R.emit("vector", "tensor_tensor",
                                       a_fm[:, jl * 1024 + tt * 512: jl * 1024 + (tt + 1) * 512],
                                       silu_t[:, bk * 512:(bk + 1) * 512], pb[4 + bk][:],
                                       op=ALU.mult,
                                       waits=[("s", s_silu), ("t", t_g)],
                                       inc=R.inc("v"))
                                hg_bank_free[bk] = ("s", s_silu)
                                hg_bank_free[4 + bk] = ("v", R.cnt["v"])
                                silu_free[bk] = ("v", R.cnt["v"])
                            w1q_free[par] = ("t", t_h)
                            w3q_free[par] = ("t", t_g)
                        v_a_tt[tt] = R.cnt["v"]
                    pi = passes.index((th, ei, hf))
                    if pi + 1 < len(passes):
                        nth, nei, nhf = passes[pi + 1]
                        w1q_dma(nei, nhf * 2 + 0, 0)
                        w3q_dma(nei, nhf * 2 + 0, 0)
                        w1q_dma(nei, nhf * 2 + 1, 1)
                        w3q_dma(nei, nhf * 2 + 1, 1)
                    t_y_last = None
                    for tl in range(8):
                        t = th * 8 + tl
                        yb = (2, 3) if tl % 2 == 0 else (6, 7)
                        for seg in range(2):
                            for jl in range(6):
                                R.emit("tensor", "matmul", pb[yb[seg]][:],
                                       a_fm[:, jl * 1024 + tl * 128: jl * 1024 + (tl + 1) * 128],
                                       w2buf[:, jl * 1024 + seg * 512: jl * 1024 + (seg + 1) * 512],
                                       start=(jl == 0), stop=(jl == 5),
                                       waits=[("w2", w2_now),
                                              ("v", v_a_tt[0] if tl < 4 else v_a_tt[1]),
                                              y_bank_free[yb[seg]]],
                                       inc=(R.inc("t") if (jl == 5 and seg == 1) else None))
                        t_y = R.cnt["t"]
                        t_y_last = t_y
                        acl = acc_ap(tl)
                        for seg in range(2):
                            R.emit("vector", "scalar_tensor_tensor",
                                   acl[:, seg * 512:(seg + 1) * 512],
                                   pb[yb[seg]][:],
                                   comb[:, t * 8 + ei: t * 8 + ei + 1],
                                   acl[:, seg * 512:(seg + 1) * 512],
                                   op0=ALU.mult, op1=ALU.add,
                                   waits=[("t", t_y), ("v", v_comb),
                                          ("v", acc_last[tl])],
                                   inc=R.inc("v"))
                            y_bank_free[yb[seg]] = ("v", R.cnt["v"])
                        acc_last[tl] = R.cnt["v"]
                    w2_free[0] = ("t", t_y_last)
                    if pi + 1 < len(passes):
                        w2_dma(passes[pi + 1][1], passes[pi + 1][2])
            for tl in range(8):
                t = th * 8 + tl
                stsem = "st0" if tl % 2 == 0 else "st1"
                stw = [("v", acc_last[tl])]
                if R.cnt[stsem]:
                    stw.append((stsem, R.cnt[stsem]))
                R.emit("gpsimd", "dma_start", out=d_out[t * 128:(t + 1) * 128, :],
                       in_=acc_ap(tl), waits=stw,
                       inc=R.inc(stsem))
                n_stores += 1
                acc_free[tl] = [(stsem, R.cnt[stsem])]

        return _finish(nc, R, ctx, taps, tap_outs, locals())


def _finish(nc, R, ctx, taps, tap_outs, env):
    if taps:
        for name in taps:
            buf = env[name]
            shape = list(buf.shape)
            dt = buf.dtype
            d = nc.dram_tensor(f"tap_{name}", shape, dt, kind="ExternalOutput").ap()
            R.emit("sync", "dma_start", out=d, in_=buf[:],
                   waits=[("t", R.cnt["t"]), ("v", R.cnt["v"]),
                          ("s", R.cnt["s"])] +
                         [(k, R.cnt[k]) for k in
                          ("st0", "st1", "ac0", "ac1", "ac2", "ac3")],
                   inc=R.inc("cst"))

    out_sems = {k: R.cnt[k] for k in ("st0", "st1", "ac0", "ac1", "ac2", "ac3")
                if R.cnt[k] > 0}
    cst_total = R.cnt["cst"]

    with nc.allow_low_precision("bf16 softmax-normalizer broadcast"), \
         nc.Block() as block:
        @block.tensor
        def _(E):
            for f in R.ops["tensor"]:
                f(E)

        @block.vector
        def _(E):
            for f in R.ops["vector"]:
                f(E)

        @block.scalar
        def _(E):
            for f in R.ops["scalar"]:
                f(E)

        @block.sync
        def _(E):
            for f in R.ops["sync"]:
                f(E)
            for k, v in out_sems.items():
                E.wait_ge(R.sems[k], v)
            E.wait_ge(R.sems["cst"], cst_total)

        @block.gpsimd
        def _(E):
            for f in R.ops["gpsimd"]:
                f(E)
            for k, v in out_sems.items():
                E.wait_ge(R.sems[k], v)

    ctx.close()
    return nc


def _prep_inputs(inputs):
    f32 = np.float32
    wqkvT = np.ascontiguousarray(np.asarray(inputs["L0_attn_Wqkv"], f32).T)
    woutT = np.ascontiguousarray(np.asarray(inputs["L0_attn_Wout"], f32).T)
    bout = np.asarray(inputs["L0_attn_bout"], f32).reshape(1, C)
    wl1T = np.ascontiguousarray(np.asarray(inputs["L0_attn_Wl1"], f32).T)
    bl1 = np.asarray(inputs["L0_attn_bl1"], f32).reshape(64, 1)
    wl2T = np.ascontiguousarray(np.asarray(inputs["L0_attn_Wl2"], f32).T)
    bl2 = np.asarray(inputs["L0_attn_bl2"], f32).reshape(16, 1)
    wr = np.ascontiguousarray(np.asarray(inputs["L0_ffn_Wr"], f32))
    w1 = np.ascontiguousarray(np.asarray(inputs["L0_ffn_W1"], f32))
    w2 = np.ascontiguousarray(np.asarray(inputs["L0_ffn_W2"], f32))
    w3 = np.ascontiguousarray(np.asarray(inputs["L0_ffn_W3"], f32))
    x_full = np.asarray(inputs["L0_attn_x"], f32)
    ident = np.eye(128, dtype=f32)
    import jax
    import jax.numpy as jnp
    logp_full = np.asarray(jax.jit(
        lambda: jnp.log(jnp.arange(4096, dtype=jnp.float32) + 1.0),
        backend="cpu")())

    in_maps = []
    for c in range(N_CORES):
        b = c // 2
        t0 = (c % 2) * 2048
        xs = np.ascontiguousarray(x_full[b, t0:t0 + 2048, :])
        logpT = logp_full[t0:t0 + 2048].reshape(1, T)
        in_maps.append(dict(
            x=xs, wqkvT=wqkvT, woutT=woutT, bout=bout, wl1T=wl1T, bl1=bl1,
            wl2T=wl2T, bl2=bl2, wr=wr, w1=w1, w2=w2, w3=w3, logpT=logpT,
            ident=ident))
    return in_maps, x_full.shape


def kernel(**inputs):
    in_maps, xshape = _prep_inputs(inputs)
    nc = build_program()
    res = run_bass_kernel_spmd(nc, in_maps, core_ids=list(range(N_CORES)))
    out = np.empty(xshape, np.float32)
    for c in range(N_CORES):
        b = c // 2
        t0 = (c % 2) * 2048
        out[b, t0:t0 + 2048, :] = res.results[c]["out"]
    return out


# version 18
# speedup vs baseline: 1.0026x; 1.0026x over previous
import numpy as np
import ml_dtypes
from contextlib import ExitStack

import concourse.bass as bass
import concourse.mybir as mybir
from concourse.bass_utils import run_bass_kernel_spmd

F32 = mybir.dt.float32
BF16 = mybir.dt.bfloat16
AF = mybir.ActivationFunctionType
ALU = mybir.AluOpType
AX = mybir.AxisListType.X

N_CORES = 8
T = 2048
C = 1024
H = 16
DH = 64
CS = 512
NG = 4
NT = 16
NE = 8
HID = 1536
EPS = 1e-5

ENGINES = ("tensor", "vector", "scalar", "sync", "gpsimd")


class Rec:

    def __init__(self):
        self.ops = {e: [] for e in ENGINES}
        self.cnt = {k: 0 for k in (
            "t", "v", "s", "g",
            "cst", "x0", "x1",
            "wv", "wqk", "wo", "w1a", "w1b", "w3a", "w3b", "w2",
            "st0", "st1", "ac0", "ac1", "ac2", "ac3")}
        self.sems = {}
        self._waited = {}

    DMA_SEMS = ("cst", "x0", "x1", "wv", "wqk", "wo", "w1a", "w1b",
                "w3a", "w3b", "w2", "st0", "st1", "ac0", "ac1", "ac2", "ac3")

    def inc(self, sem, n=1):
        if sem in self.DMA_SEMS:
            n *= 16
        self.cnt[sem] += n
        return (sem, self.cnt[sem], n)

    OWN_SEM = {"tensor": "t", "vector": "v", "scalar": "s", "gpsimd": "g"}

    def emit(self, eng, method, *args, waits=(), inc=None, **kw):
        ws = []
        own = self.OWN_SEM.get(eng)
        for (s, v) in waits:
            if v <= 0:
                continue
            key = (eng, s)
            if s != own and self._waited.get(key, -1) >= v:
                continue
            self._waited[key] = v
            ws.append((s, v))
        ws = tuple(ws)
        incs = None if inc is None else (inc[0], inc[2])

        def f(E, method=method, args=args, kw=kw, ws=ws, incs=incs):
            for s, v in ws:
                E.wait_ge(self.sems[s], v)
            r = getattr(E, method)(*args, **kw)
            if incs is not None:
                r.then_inc(self.sems[incs[0]], incs[1])
        self.ops[eng].append(f)

    def wait(self, eng, waits):
        return waits


def build_program(n_phases=7, taps=()):
    nc = bass.Bass("TRN2", target_bir_lowering=False)

    d_x = nc.dram_tensor("x", [T, C], F32, kind="ExternalInput").ap()
    d_wqkvT = nc.dram_tensor("wqkvT", [C, 3 * C], F32, kind="ExternalInput").ap()
    d_woutT = nc.dram_tensor("woutT", [C, C], F32, kind="ExternalInput").ap()
    d_bout = nc.dram_tensor("bout", [1, C], F32, kind="ExternalInput").ap()
    d_wl1T = nc.dram_tensor("wl1T", [1, 64], F32, kind="ExternalInput").ap()
    d_bl1 = nc.dram_tensor("bl1", [64, 1], F32, kind="ExternalInput").ap()
    d_wl2T = nc.dram_tensor("wl2T", [64, 16], F32, kind="ExternalInput").ap()
    d_bl2 = nc.dram_tensor("bl2", [16, 1], F32, kind="ExternalInput").ap()
    d_wr = nc.dram_tensor("wr", [C, NE], F32, kind="ExternalInput").ap()
    d_w1 = nc.dram_tensor("w1", [NE, C, HID], F32, kind="ExternalInput").ap()
    d_w2 = nc.dram_tensor("w2", [NE, HID, C], F32, kind="ExternalInput").ap()
    d_w3 = nc.dram_tensor("w3", [NE, C, HID], F32, kind="ExternalInput").ap()
    d_logpT = nc.dram_tensor("logpT", [1, T], F32, kind="ExternalInput").ap()
    d_ident = nc.dram_tensor("ident", [128, 128], F32, kind="ExternalInput").ap()
    d_out = nc.dram_tensor("out", [T, C], F32, kind="ExternalOutput").ap()

    tap_outs = {}

    with ExitStack() as ctx:
        e = ctx.enter_context

        ARENA = 115200
        arena = nc.alloc_sbuf_tensor("arena", [128, ARENA], mybir.dt.uint8)
        abase = nc.lookup_mloc(arena).addr

        def place(name, shape, dtype, off):
            nbytes = int(np.prod(shape[1:])) * mybir.dt.size(dtype)
            assert off + nbytes <= ARENA, (name, off, nbytes)
            return nc.alloc_sbuf_tensor_at(name, shape, dtype, offset=abase + off)

        k_fm = place("k_fm", [128, 16384], BF16, 0)
        logp_sb = place("logp_sb", [1, T], F32, 0)
        h1_sb = place("h1_sb", [64, T], F32, 8192)
        lam_fm = place("lam_fm", [16, T], F32, 16384)
        woutT_b = place("woutT_b", [128, 8192], BF16, 98816)
        accA = place("accA", [128, 2048], F32, 16384)
        x_slot = place("x_slot", [128, 2048], F32, 24576)
        a_fm = place("a_fm", [128, 6144], BF16, 0)
        v_sc = place("v_sc", [128, 16640], BF16, 32768)
        xn_f32 = place("xn_f32", [128, 2048], F32, 32768)
        xnT_f32 = place("xnT_f32", [128, 2048], F32, 40960)
        w2buf = place("w2buf", [128, 6144], BF16, 32768)
        w1q = place("w1q", [128, 6144], BF16, 45056)
        w3q = place("w3q", [128, 6144], BF16, 57344)
        wqkvT_b = place("wqkvT_b", [128, 16384], BF16, 66048)
        accB = place("accB", [128, 4096], F32, 69632)
        silu_t = place("silu_t", [128, 1024], F32, 86016)
        accC = place("accC", [128, 2048], F32, 90112)
        exp_sb = place("exp_sb", [128, 8192], BF16, 69632)
        wvT_b = place("wvT_b", [128, 8192], BF16, 98816)

        xn_fm = e(nc.sbuf_tensor("xn_fm", [128, 16384], BF16))
        q_fm = e(nc.sbuf_tensor("q_fm", [128, 16384], BF16))
        lamT = e(nc.sbuf_tensor("lamT", [128, 256], F32))
        ident_sb = e(nc.sbuf_tensor("ident_sb", [128, 128], F32))
        ones_sb = e(nc.sbuf_tensor("ones_sb", [128, 128], F32))
        ones_bf = e(nc.sbuf_tensor("ones_bf", [128, 64], BF16))
        wr_sb = e(nc.sbuf_tensor("wr_sb", [128, 64], F32))
        bout_r = e(nc.sbuf_tensor("bout_r", [1, C], F32))
        boutB = e(nc.sbuf_tensor("boutB", [128, C], F32))
        comb = e(nc.sbuf_tensor("comb", [128, 128], F32))
        logits = e(nc.sbuf_tensor("logits", [128, 128], F32))
        xsq = e(nc.sbuf_tensor("xsq", [128, C], F32))
        stat = e(nc.sbuf_tensor("stat", [128, 32], F32))
        cscr = e(nc.sbuf_tensor("cscr", [128, 64], F32))
        rcp_sb = e(nc.sbuf_tensor("rcp_sb", [128, 2048], BF16))
        ln_st = e(nc.sbuf_tensor("ln_st", [1, 1024], F32))
        rB_sb = e(nc.sbuf_tensor("rB_sb", [64, 1024], F32))
        wl1T_sb = e(nc.sbuf_tensor("wl1T_sb", [1, 64], F32))
        bl1_sb = e(nc.sbuf_tensor("bl1_sb", [64, 1], F32))
        wl2T_sb = e(nc.sbuf_tensor("wl2T_sb", [64, 16], F32))
        bl2_sb = e(nc.sbuf_tensor("bl2_sb", [16, 1], F32))

        pb = [e(nc.psum_tensor(f"pb{i}", [128, 512], F32)) for i in range(8)]

        R = Rec()
        sem_handles = {}
        for name in R.cnt:
            sem_handles[name] = e(nc.semaphore(f"sem_{name}"))
        R.sems = sem_handles

        v_view = v_sc[:].rearrange("p (ti h d) -> p ti h d", ti=16, h=16)

        R.emit("gpsimd", "memset", ones_sb[:], 1.0, inc=R.inc("g"))
        R.emit("gpsimd", "memset", ones_bf[:], 1.0, inc=R.inc("g"))
        g_ones = R.cnt["g"]
        R.emit("gpsimd", "dma_start",
               out=wvT_b[:].rearrange("p (K f) -> p K f", K=8),
               in_=d_wqkvT[:, 2048:3072].rearrange("(K p) f -> p K f", p=128),
               inc=R.inc("wv"))
        wd_wave2 = R.cnt["wv"]
        R.emit("gpsimd", "dma_start",
               out=wqkvT_b[:].rearrange("p (K f) -> p K f", K=8),
               in_=d_wqkvT[:, 0:2048].rearrange("(K p) f -> p K f", p=128),
               inc=R.inc("wqk"))
        wd_wave1 = R.cnt["wqk"]

        for ap_out, ap_in in (
            (logp_sb[:], d_logpT), (wl1T_sb[:], d_wl1T), (bl1_sb[:], d_bl1),
            (wl2T_sb[:], d_wl2T), (bl2_sb[:], d_bl2), (ident_sb[:], d_ident),
            (wr_sb[:].rearrange("p (K n) -> p K n", K=8),
             d_wr.rearrange("(K p) n -> p K n", p=128)),
            (bout_r[:], d_bout),
        ):
            R.emit("sync", "dma_start", out=ap_out, in_=ap_in, inc=R.inc("cst"))
        xd_consts = R.cnt["cst"]

        lam_sig_tick = []
        for i in range(4):
            R.emit("tensor", "matmul", pb[i % 2][0:64, :], wl1T_sb[0:1, :],
                   logp_sb[0:1, i * 512:(i + 1) * 512], start=True, stop=True,
                   waits=[("cst", xd_consts), ("s", R.cnt["s"])],
                   inc=R.inc("t"))
            R.emit("scalar", "activation", h1_sb[0:64, i * 512:(i + 1) * 512],
                   pb[i % 2][0:64, :], AF.Relu, bias=bl1_sb[0:64, 0:1],
                   waits=[("t", R.cnt["t"])], inc=R.inc("s"))
        for i in range(4):
            R.emit("tensor", "matmul", pb[2 + i % 2][0:16, :], wl2T_sb[0:64, :],
                   h1_sb[0:64, i * 512:(i + 1) * 512], start=True, stop=True,
                   waits=[("s", R.cnt["s"])], inc=R.inc("t"))
            R.emit("scalar", "activation", lam_fm[0:16, i * 512:(i + 1) * 512],
                   pb[2 + i % 2][0:16, :], AF.Sigmoid, bias=bl2_sb[0:16, 0:1],
                   waits=[("t", R.cnt["t"])], inc=R.inc("s"))
            lam_sig_tick.append(R.cnt["s"])
        for i in range(16):
            R.emit("tensor", "matmul", pb[4][:, i * 16:(i + 1) * 16],
                   lam_fm[0:16, i * 128:(i + 1) * 128], ident_sb[0:16, 0:16],
                   is_transpose=True, start=True, stop=True,
                   waits=[("s", lam_sig_tick[i // 4])],
                   inc=(R.inc("t") if i == 15 else None))
        t_lamT = R.cnt["t"]
        R.emit("vector", "tensor_scalar", lamT[:, 0:256], pb[4][:, 0:256],
               0.125, None, op0=ALU.mult, waits=[("t", t_lamT)], inc=R.inc("v"))
        v_lamT = R.cnt["v"]

        if n_phases < 2:
            return _finish(nc, R, ctx, taps, tap_outs, locals())

        xn_fm_v = xn_fm[:].rearrange("p (c m) -> p c m", c=8)
        xslot_free = [[("v", 0), ("s", 0)], [("v", 0), ("s", 0)]]
        xsq_free = ("v", 0)
        tp_bank_free = [[("v", 0), ("s", 0)], [("v", 0), ("s", 0)]]
        gate_bank_free = [("v", 0), ("v", 0)]
        xnf_free = [("t", 0), ("t", 0)]
        xnT_free = [("t", 0), ("t", 0)]
        tp_tick = {}
        evac_tick = {}
        sv_bank_free = 0
        logits_tick = {}
        xn_tick = {}
        xn_done_tick = []
        for j in range(NT + 2):
            sl = j % 2
            if j < NT:
                xsem = "x0" if sl == 0 else "x1"
                R.emit("sync", "dma_start", out=x_slot[:, sl * 1024:(sl + 1) * 1024],
                       in_=d_x[j * 128:(j + 1) * 128, :],
                       waits=xslot_free[sl], inc=R.inc(xsem))
                xw = [(xsem, R.cnt[xsem])]
                xsl = x_slot[:, sl * 1024:(sl + 1) * 1024]
                sb = sl * 16
                R.emit("vector", "reduce_sum", stat[:, sb:sb + 1], xsl, AX,
                       waits=xw + [("s", R.cnt["s"])])
                R.emit("scalar", "activation", xsq[:], xsl, AF.Square,
                       waits=xw + [xsq_free], inc=R.inc("s"))
                R.emit("vector", "reduce_sum", stat[:, sb + 1:sb + 2], xsq[:], AX,
                       waits=[("s", R.cnt["s"])], inc=R.inc("v"))
                xsq_free = ("v", R.cnt["v"])
                v_r2 = R.cnt["v"]
                R.emit("vector", "tensor_scalar", stat[:, sb + 5:sb + 6],
                       stat[:, sb:sb + 1], 1.0 / C, None, op0=ALU.mult,
                       waits=[("v", v_r2)], inc=R.inc("v"))
                R.emit("vector", "tensor_tensor", stat[:, sb + 3:sb + 4],
                       stat[:, sb + 5:sb + 6], stat[:, sb + 5:sb + 6],
                       op=ALU.mult, waits=[("v", R.cnt["v"])])
                R.emit("vector", "tensor_scalar", stat[:, sb + 2:sb + 3],
                       stat[:, sb + 1:sb + 2], 1.0 / C, None, op0=ALU.mult,
                       waits=[("v", v_r2)], inc=R.inc("v"))
                R.emit("vector", "tensor_tensor", stat[:, sb + 4:sb + 5],
                       stat[:, sb + 2:sb + 3], stat[:, sb + 3:sb + 4],
                       op=ALU.subtract, waits=[("v", R.cnt["v"])], inc=R.inc("v"))
                R.emit("vector", "tensor_scalar", stat[:, sb + 8:sb + 9],
                       stat[:, sb + 4:sb + 5], EPS, None, op0=ALU.add,
                       waits=[("v", R.cnt["v"])], inc=R.inc("v"))
                v_var = R.cnt["v"]
                R.emit("scalar", "activation", stat[:, sb + 6:sb + 7],
                       stat[:, sb + 8:sb + 9], AF.Sqrt, waits=[("v", v_var)],
                       inc=R.inc("s"))
                R.emit("vector", "reciprocal", stat[:, sb + 7:sb + 8],
                       stat[:, sb + 6:sb + 7], waits=[("s", R.cnt["s"])],
                       inc=R.inc("v"))
                R.emit("vector", "tensor_scalar",
                       xn_f32[:, sl * 1024:(sl + 1) * 1024], xsl,
                       stat[:, sb + 5:sb + 6], stat[:, sb + 7:sb + 8],
                       op0=ALU.subtract, op1=ALU.mult,
                       waits=[("v", R.cnt["v"]), xnf_free[sl]], inc=R.inc("v"))
                xn_tick[j] = R.cnt["v"]
                xslot_free[sl] = [("v", xn_tick[j]), ("s", R.cnt["s"])]
            i = j - 1
            if 0 <= i < NT:
                isl = i % 2
                bk = isl * 3
                for c in range(8):
                    R.emit("tensor", "matmul",
                           pb[bk + c // 4][:, (c % 4) * 128:(c % 4 + 1) * 128],
                           xn_f32[:, isl * 1024 + c * 128: isl * 1024 + (c + 1) * 128],
                           ident_sb[:], is_transpose=True, start=True, stop=True,
                           waits=[("v", xn_tick[i])] + tp_bank_free[isl],
                           inc=(R.inc("t") if c == 7 else None))
                tp_tick[i] = R.cnt["t"]
                xnf_free[isl] = ("t", tp_tick[i])
            i2 = j - 2
            if 0 <= i2 < NT:
                i2sl = i2 % 2
                bk2 = i2sl * 3 + 2
                for K in range(8):
                    R.emit("tensor", "matmul", pb[bk2][:, 0:8],
                           xnT_f32[:, i2sl * 1024 + K * 128: i2sl * 1024 + (K + 1) * 128],
                           wr_sb[:, K * 8:(K + 1) * 8],
                           start=(K == 0), stop=(K == 7),
                           waits=[("s", evac_tick[i2][1]),
                                  ("v", evac_tick[i2][0]), gate_bank_free[i2sl]],
                           inc=(R.inc("t") if K == 7 else None))
                xnT_free[i2sl] = ("t", R.cnt["t"])
                gate_tick = R.cnt["t"]
            if 0 <= i < NT:
                isl = i % 2
                bk = isl * 3
                xnT_v = xnT_f32[:, isl * 1024:(isl + 1) * 1024].rearrange(
                    "p (c m) -> p c m", c=8)
                psv0 = pb[bk][:].rearrange("p (c m) -> p c m", c=4)
                psv1 = pb[bk + 1][:].rearrange("p (c m) -> p c m", c=4)
                R.emit("vector", "tensor_copy",
                       out=xn_fm_v[:, 0:4, i * 128:(i + 1) * 128],
                       in_=psv0, waits=[("t", tp_tick[i])])
                R.emit("vector", "tensor_copy", out=xnT_v[:, 0:4, :],
                       in_=psv0, waits=[xnT_free[isl]], inc=R.inc("v"))
                R.emit("scalar", "activation",
                       out=xn_fm_v[:, 4:8, i * 128:(i + 1) * 128],
                       in_=psv1, func=AF.Copy,
                       waits=[("t", tp_tick[i]), xnT_free[isl]])
                R.emit("scalar", "activation", out=xnT_v[:, 4:8, :],
                       in_=psv1, func=AF.Copy, inc=R.inc("s"))
                evac_tick[i] = (R.cnt["v"], R.cnt["s"])
                tp_bank_free[isl] = [("v", R.cnt["v"]), ("s", R.cnt["s"])]
                xn_done_tick.append(R.cnt["v"])
            if 0 <= i2 < NT:
                R.emit("vector", "tensor_copy", out=logits[:, i2 * 8:(i2 + 1) * 8],
                       in_=pb[i2 % 2 * 3 + 2][:, 0:8], waits=[("t", gate_tick)],
                       inc=R.inc("v"))
                gate_bank_free[i2 % 2] = ("v", R.cnt["v"])
                logits_tick[i2] = R.cnt["v"]
            tv = j - 2
            if 8 <= tv < NT:
                for K in range(8):
                    for seg in range(2):
                        R.emit("tensor", "matmul", pb[6 + seg][:],
                               xn_fm[:, K * 2048 + tv * 128: K * 2048 + (tv + 1) * 128],
                               wvT_b[:, K * 1024 + seg * 512: K * 1024 + (seg + 1) * 512],
                               start=(K == 0), stop=(K == 7),
                               waits=[("wv", wd_wave2), ("v", evac_tick[tv][0]),
                                      ("s", max(evac_tick[tv][1], sv_bank_free))],
                               inc=(R.inc("t") if (K == 7 and seg == 1) else None))
                t_v2 = R.cnt["t"]
                for seg in range(2):
                    R.emit("scalar", "activation",
                           v_view[:, tv, seg * 8:(seg + 1) * 8, 0:64]
                           if False else
                           v_view[:, tv, seg * 8:(seg + 1) * 8, 0:64],
                           pb[6 + seg][:].rearrange("p (h d) -> p h d", h=8),
                           AF.Copy, waits=[("t", t_v2)],
                           inc=(R.inc("s") if seg == 1 else None))
                sv_bank_free = R.cnt["s"]
        v_xn_all = R.cnt["v"]
        s_vP2_all = R.cnt["s"]
        xn_all_done = [("v", R.cnt["v"]), ("s", R.cnt["s"])]
        R.emit("gpsimd", "memset", v_view[:, :, :, 64:65], 1.0,
               waits=xn_all_done, inc=R.inc("g"))
        g_ones = R.cnt["g"]


        if n_phases < 3:
            return _finish(nc, R, ctx, taps, tap_outs, locals())

        def emit_comb(t):
                L = logits[:, t * 8:(t + 1) * 8]
                m1, nm1, m2, em, den, rden = (cscr[:, 32 + j:33 + j] for j in range(6))
                ge1 = cscr[:, 0:8]
                mskt = cscr[:, 8:16]
                e1 = cscr[:, 16:24]
                mask2 = cscr[:, 24:32]
                R.emit("vector", "reduce_max", m1, L, AX,
                       waits=[("s", R.cnt["s"]), ("v", logits_tick[t])],
                       inc=R.inc("v"))
                R.emit("vector", "reduce_max", nm1, L, AX, negate=True)
                R.emit("vector", "tensor_scalar", ge1, L, m1, None, op0=ALU.is_ge,
                       waits=[("v", R.cnt["v"])], inc=R.inc("v"))
                R.emit("vector", "tensor_scalar", ge1, ge1, -1e30, None,
                       op0=ALU.mult, waits=[("v", R.cnt["v"])], inc=R.inc("v"))
                R.emit("vector", "tensor_tensor", mskt, L, ge1, op=ALU.add,
                       waits=[("v", R.cnt["v"])], inc=R.inc("v"))
                R.emit("vector", "reduce_max", m2, mskt, AX,
                       waits=[("v", R.cnt["v"])], inc=R.inc("v"))
                v_m2 = R.cnt["v"]
                R.emit("scalar", "activation", e1, L, AF.Exp, bias=nm1,
                       waits=[("v", v_m2)])
                R.emit("scalar", "activation", em, m2, AF.Exp, bias=nm1,
                       inc=R.inc("s"))
                s_exp = R.cnt["s"]
                R.emit("vector", "tensor_scalar", mask2, L, m2, None, op0=ALU.is_ge,
                       waits=[("s", s_exp)])
                R.emit("vector", "tensor_scalar", den, em, 1.0, None, op0=ALU.add,
                       inc=R.inc("v"))
                R.emit("vector", "reciprocal", rden, den,
                       waits=[("v", R.cnt["v"])])
                R.emit("vector", "tensor_tensor", e1, e1, mask2, op=ALU.mult,
                       waits=[("v", R.cnt["v"])], inc=R.inc("v"))
                R.emit("vector", "tensor_scalar", comb[:, t * 8:(t + 1) * 8], e1,
                       rden, None, op0=ALU.mult, waits=[("v", R.cnt["v"])],
                       inc=R.inc("v"))
        qk_bank_free = [("v", 0), ("v", 0)]
        for fc in range(16):
            banks = [0, 1, 2, 3] if fc % 2 == 0 else [4, 5, 6, 7]
            qk_extra = [("s", s_vP2_all)] if fc % 2 == 1 else []
            for K in range(8):
                for tt in range(4):
                    R.emit("tensor", "matmul", pb[banks[tt]][:],
                           wqkvT_b[:, K * 2048 + fc * 128: K * 2048 + (fc + 1) * 128],
                           xn_fm[:, K * 2048 + tt * 512: K * 2048 + (tt + 1) * 512],
                           start=(K == 0), stop=(K == 7),
                           waits=[("wqk", wd_wave1), qk_bank_free[fc % 2]]
                                 + xn_all_done + qk_extra,
                           inc=(R.inc("t") if (K == 7 and tt == 3) else None))
            t_fc = R.cnt["t"]
            dst = q_fm if fc < 8 else k_fm
            cc = fc % 8
            for tt in range(4):
                R.emit("vector", "tensor_copy",
                       out=dst[:, cc * 2048 + tt * 512: cc * 2048 + (tt + 1) * 512],
                       in_=pb[banks[tt]][:], waits=[("t", t_fc)],
                       inc=(R.inc("v") if tt == 3 else None))
            qk_bank_free[fc % 2] = ("v", R.cnt["v"])
            emit_comb(fc)
        v_qk_all = R.cnt["v"]
        v_comb = R.cnt["v"]

        v_bank_free = [("v", v_qk_all), ("v", v_qk_all)]
        for t in range(8):
            b0 = 0 if t % 2 == 0 else 2
            for K in range(8):
                for seg in range(2):
                    R.emit("tensor", "matmul", pb[b0 + seg][:],
                           xn_fm[:, K * 2048 + t * 128: K * 2048 + (t + 1) * 128],
                           wvT_b[:, K * 1024 + seg * 512: K * 1024 + (seg + 1) * 512],
                           start=(K == 0), stop=(K == 7),
                           waits=[("wv", wd_wave2), v_bank_free[t % 2],
                                  ("g", g_ones)],
                           inc=(R.inc("t") if (K == 7 and seg == 1) else None))
            t_v = R.cnt["t"]
            for seg in range(2):
                R.emit("vector", "tensor_copy",
                       out=v_view[:, t, seg * 8:(seg + 1) * 8, 0:64],
                       in_=pb[b0 + seg][:].rearrange("p (h d) -> p h d", h=8),
                       waits=[("t", t_v)],
                       inc=(R.inc("v") if seg == 1 else None))
            v_bank_free[t % 2] = ("v", R.cnt["v"])
        v_v_all = R.cnt["v"]
        t_v_mm_all = R.cnt["t"]
        R.emit("gpsimd", "dma_start",
               out=woutT_b[:].rearrange("p (K f) -> p K f", K=8),
               in_=d_woutT.rearrange("(K p) f -> p K f", p=128),
               waits=[("t", t_v_mm_all)], inc=R.inc("wo"))
        wd_woutT = R.cnt["wo"]

        for seg in range(2):
            R.emit("tensor", "matmul", pb[4 + seg][:],
                   ones_sb[0:1, 0:128], bout_r[0:1, seg * 512:(seg + 1) * 512],
                   start=True, stop=True,
                   waits=[("cst", xd_consts), ("v", v_v_all), ("g", g_ones)],
                   inc=(R.inc("t") if seg == 1 else None))
        R.emit("vector", "tensor_copy", out=boutB[:, 0:512], in_=pb[4][:],
               waits=[("t", R.cnt["t"])])
        R.emit("vector", "tensor_copy", out=boutB[:, 512:1024], in_=pb[5][:],
               inc=R.inc("v"))
        v_boutB = R.cnt["v"]

        if n_phases < 4:
            return _finish(nc, R, ctx, taps, tap_outs, locals())

        NI = NG * H

        def _gh(i):
            return i // H, i % H

        sT_bank_free = [("s", 0)] * 4
        av_slot_free = [("v", v_boutB), ("v", v_boutB), ("v", v_v_all)]
        n_bank_free = ("s", 0)
        rcp_free = [("t", 0), ("t", 0)]
        rb_free = ("v", 0)
        exp_free = [("t", 0), ("t", 0)]
        exp_ticks = {}
        av_ticks = {}
        rcp_ticks = {}
        bc_ticks = {}
        rb_ticks = {}
        t_sT_all = None

        for j in range(NI + 2):
            if j < NI:
                g, h = _gh(j)
                hc, hr = h // 2, (h % 2) * 64
                for kc in range(4):
                    R.emit("tensor", "matmul", pb[kc][:],
                           k_fm[hr:hr + 64,
                                hc * 2048 + g * 512 + kc * 128: hc * 2048 + g * 512 + (kc + 1) * 128],
                           q_fm[hr:hr + 64, hc * 2048 + g * 512: hc * 2048 + (g + 1) * 512],
                           start=True, stop=True,
                           waits=[sT_bank_free[kc], ("v", v_v_all)],
                           inc=R.inc("t"))
                    exp_ticks.setdefault(j, []).append(None)
                    exp_ticks[j][kc] = ("t_sT", R.cnt["t"])
                if j == NI - 1:
                    t_sT_all = R.cnt["t"]
            if j < NI:
                g, h = _gh(j)
                par = j % 2
                for kc in range(4):
                    R.emit("scalar", "activation",
                           exp_sb[:, par * 2048 + kc * 512: par * 2048 + (kc + 1) * 512],
                           pb[kc][:], AF.Exp,
                           scale=lamT[:, (g * 4 + kc) * 16 + h: (g * 4 + kc) * 16 + h + 1],
                           waits=[("t", exp_ticks[j][kc][1]), ("v", v_lamT),
                                  exp_free[par]],
                           inc=R.inc("s"))
                    sT_bank_free[kc] = ("s", R.cnt["s"])
                    exp_ticks[j][kc] = ("s", R.cnt["s"])
            i = j - 1
            if 0 <= i < NI:
                g, h = _gh(i)
                par = i % 2
                bk = 4 + i % 3
                for kc in range(4):
                    R.emit("tensor", "matmul", pb[bk][0:65, :],
                           v_view[:, g * 4 + kc, h, :],
                           exp_sb[:, par * 2048 + kc * 512: par * 2048 + (kc + 1) * 512],
                           start=(kc == 0), stop=(kc == 3),
                           waits=[exp_ticks[i][kc], ("v", v_v_all),
                                  ("s", s_vP2_all), ("g", g_ones)] +
                                 ([av_slot_free[i % 3]] if kc == 0 else []),
                           inc=(R.inc("t") if kc == 3 else None))
                av_ticks[i] = R.cnt["t"]
                exp_free[par] = ("t", av_ticks[i])
            if 0 <= i < NI:
                R.emit("vector", "reciprocal", rcp_sb[64:65, (i % 2) * 512:(i % 2) * 512 + 512],
                       pb[4 + i % 3][64:65, :],
                       waits=[("t", av_ticks[i]), rcp_free[i % 2]],
                       inc=R.inc("v"))
                rcp_ticks[i] = R.cnt["v"]
            i2 = j - 2
            if 0 <= i2 < NI:
                R.emit("tensor", "matmul", pb[7][0:64, :],
                       ones_bf[64:65, 0:64],
                       rcp_sb[64:65, (i2 % 2) * 512:(i2 % 2) * 512 + 512],
                       start=True, stop=True,
                       waits=[("v", rcp_ticks[i2]), n_bank_free, ("g", g_ones)],
                       inc=R.inc("t"))
                bc_ticks[i2] = R.cnt["t"]
                rcp_free[i2 % 2] = ("t", bc_ticks[i2])
            if 0 <= i2 < NI:
                R.emit("scalar", "activation", rB_sb[0:64, 0:512], pb[7][0:64, :],
                       AF.Copy, waits=[("t", bc_ticks[i2]), rb_free],
                       inc=R.inc("s"))
                rb_ticks[i2] = R.cnt["s"]
                n_bank_free = ("s", rb_ticks[i2])
            if 0 <= i2 < NI:
                g, h = _gh(i2)
                hc, hr = h // 2, (h % 2) * 64
                R.emit("vector", "tensor_tensor",
                       q_fm[hr:hr + 64, hc * 2048 + g * 512: hc * 2048 + (g + 1) * 512],
                       rB_sb[0:64, 0:512], pb[4 + i2 % 3][0:64, :], op=ALU.mult,
                       waits=[("s", rb_ticks[i2]), ("v", v_qk_all)],
                       inc=R.inc("v"))
                av_slot_free[i2 % 3] = ("v", R.cnt["v"])
                rb_free = ("v", R.cnt["v"])
        v_ao_all = R.cnt["v"]
        t_av_all = R.cnt["t"]

        if n_phases < 6:
            return _finish(nc, R, ctx, taps, tap_outs, locals())

        def acc_ap(tl):
            if tl < 2:
                return accA[:, tl * 1024:(tl + 1) * 1024]
            if tl < 6:
                return accB[:, (tl - 2) * 1024:(tl - 1) * 1024]
            return accC[:, (tl - 6) * 1024:(tl - 5) * 1024]

        def w1q_dma(ei, Q, par):
            R.emit("gpsimd", "dma_start",
                   out=w1q[:, par * 3072:(par + 1) * 3072].rearrange("p (K h) -> p K h", K=8),
                   in_=d_w1[ei, :, Q * 384:(Q + 1) * 384].rearrange("(K p) h -> p K h", p=128),
                   waits=[w1q_free[par]], inc=R.inc("w1a" if par == 0 else "w1b"))
            w1q_tick[par] = R.cnt["w1a" if par == 0 else "w1b"]

        def w3q_dma(ei, Q, par):
            R.emit("gpsimd", "dma_start",
                   out=w3q[:, par * 3072:(par + 1) * 3072].rearrange("p (K h) -> p K h", K=8),
                   in_=d_w3[ei, :, Q * 384:(Q + 1) * 384].rearrange("(K p) h -> p K h", p=128),
                   waits=[w3q_free[par]], inc=R.inc("w3a" if par == 0 else "w3b"))
            w3q_tick[par] = R.cnt["w3a" if par == 0 else "w3b"]

        def w2_dma(ei, hf):
            R.emit("gpsimd", "dma_start",
                   out=w2buf[:].rearrange("p (J c) -> p J c", J=6),
                   in_=d_w2[ei, hf * 768:(hf + 1) * 768, :].rearrange("(J p) c -> p J c", p=128),
                   waits=[w2_free[0]], inc=R.inc("w2"))
            w2_tick[0] = R.cnt["w2"]

        w1q_free = [("t", t_av_all), ("t", t_av_all)]
        w3q_free = [("t", t_av_all), ("t", t_av_all)]
        w2_free = [("t", t_av_all)]
        w1q_tick = [0, 0]
        w3q_tick = [0, 0]
        w2_tick = [0]
        passes = [(th, ei, hf) for th in range(2) for ei in range(NE)
                  for hf in range(2)]
        w2_dma(0, 0)
        w1q_dma(0, 0, 0)
        w3q_dma(0, 0, 0)
        w1q_dma(0, 1, 1)
        w3q_dma(0, 1, 1)

        wout_bank_free = [[("v", v_boutB)], [("v", v_boutB)]]
        xslot_free2 = [[("t", t_sT_all)], [("t", t_sT_all)]]
        acc_free = [[] for _ in range(8)]
        hg_bank_free = [("v", 0)] * 8
        y_bank_free = {2: ("v", 0), 3: ("v", 0), 6: ("v", 0), 7: ("v", 0)}
        silu_free = [("v", 0), ("v", 0)]
        acc_last = [0] * 8
        n_stores = 0

        for th in range(2):
            if th == 1:
                wout_bank_free = [[("v", R.cnt["v"]), ("s", R.cnt["s"])]] * 2
            for tl in range(8):
                t = th * 8 + tl
                sl = tl % 2
                b0 = sl * 2
                xsem = "x0" if sl == 0 else "x1"
                R.emit("sync", "dma_start", out=x_slot[:, sl * 1024:(sl + 1) * 1024],
                       in_=d_x[t * 128:(t + 1) * 128, :],
                       waits=xslot_free2[sl], inc=R.inc(xsem))
                xd_x2 = R.cnt[xsem]
                for K in range(8):
                    for seg in range(2):
                        R.emit("tensor", "matmul", pb[b0 + seg][:],
                               q_fm[:, K * 2048 + t * 128: K * 2048 + (t + 1) * 128],
                               woutT_b[:, K * 1024 + seg * 512: K * 1024 + (seg + 1) * 512],
                               start=(K == 0), stop=(K == 7),
                               waits=[("wo", wd_woutT), ("v", v_ao_all)]
                                     + wout_bank_free[sl],
                               inc=(R.inc("t") if (K == 7 and seg == 1) else None))
                t_wout = R.cnt["t"]
                acl = acc_ap(tl)
                for seg in range(2):
                    R.emit("vector", "tensor_tensor",
                           acl[:, seg * 512:(seg + 1) * 512],
                           boutB[:, seg * 512:(seg + 1) * 512],
                           pb[b0 + seg][:], op=ALU.add,
                           waits=[("t", t_wout), ("v", v_boutB)] + acc_free[tl],
                           inc=R.inc("v"))
                wout_bank_free[sl] = [("v", R.cnt["v"])]
                R.emit("vector", "tensor_tensor", acl, acl,
                       x_slot[:, sl * 1024:(sl + 1) * 1024], op=ALU.add,
                       waits=[(xsem, xd_x2), ("v", R.cnt["v"])], inc=R.inc("v"))
                acc_last[tl] = R.cnt["v"]
                xslot_free2[sl] = [("v", R.cnt["v"])]
            hg_bank_free = [("v", R.cnt["v"])] * 8
            for b in (2, 3, 6, 7):
                y_bank_free[b] = ("v", R.cnt["v"])
            for ei in range(NE):
                for hf in range(2):
                    w2_now = w2_tick[0]
                    v_a_tt = [0, 0]
                    for tt in range(2):
                        for ql in range(2):
                            Q = hf * 2 + ql
                            par = Q % 2
                            w1_now, w3_now = w1q_tick[par], w3q_tick[par]
                            w1sem = "w1a" if par == 0 else "w1b"
                            w3sem = "w3a" if par == 0 else "w3b"
                            for jj in range(3):
                                jl = ql * 3 + jj
                                bk = jl % 2
                                for K in range(8):
                                    R.emit("tensor", "matmul", pb[bk][:],
                                           w1q[:, par * 3072 + K * 384 + jj * 128:
                                               par * 3072 + K * 384 + (jj + 1) * 128],
                                           xn_fm[:, K * 2048 + th * 1024 + tt * 512:
                                                 K * 2048 + th * 1024 + (tt + 1) * 512],
                                           start=(K == 0), stop=(K == 7),
                                           waits=[(w1sem, w1_now), hg_bank_free[bk]],
                                           inc=(R.inc("t") if K == 7 else None))
                                t_h = R.cnt["t"]
                                for K in range(8):
                                    R.emit("tensor", "matmul", pb[4 + bk][:],
                                           w3q[:, par * 3072 + K * 384 + jj * 128:
                                               par * 3072 + K * 384 + (jj + 1) * 128],
                                           xn_fm[:, K * 2048 + th * 1024 + tt * 512:
                                                 K * 2048 + th * 1024 + (tt + 1) * 512],
                                           start=(K == 0), stop=(K == 7),
                                           waits=[(w3sem, w3_now), hg_bank_free[4 + bk]],
                                           inc=(R.inc("t") if K == 7 else None))
                                t_g = R.cnt["t"]
                                R.emit("scalar", "activation",
                                       silu_t[:, bk * 512:(bk + 1) * 512], pb[bk][:],
                                       AF.Silu, waits=[("t", t_h), silu_free[bk]],
                                       inc=R.inc("s"))
                                s_silu = R.cnt["s"]
                                R.emit("vector", "tensor_tensor",
                                       a_fm[:, jl * 1024 + tt * 512: jl * 1024 + (tt + 1) * 512],
                                       silu_t[:, bk * 512:(bk + 1) * 512], pb[4 + bk][:],
                                       op=ALU.mult,
                                       waits=[("s", s_silu), ("t", t_g)],
                                       inc=R.inc("v"))
                                hg_bank_free[bk] = ("s", s_silu)
                                hg_bank_free[4 + bk] = ("v", R.cnt["v"])
                                silu_free[bk] = ("v", R.cnt["v"])
                            w1q_free[par] = ("t", t_h)
                            w3q_free[par] = ("t", t_g)
                        v_a_tt[tt] = R.cnt["v"]
                    pi = passes.index((th, ei, hf))
                    if pi + 1 < len(passes):
                        nth, nei, nhf = passes[pi + 1]
                        w1q_dma(nei, nhf * 2 + 0, 0)
                        w3q_dma(nei, nhf * 2 + 0, 0)
                        w1q_dma(nei, nhf * 2 + 1, 1)
                        w3q_dma(nei, nhf * 2 + 1, 1)
                    t_y_last = None
                    for tl in range(8):
                        t = th * 8 + tl
                        yb = (2, 3) if tl % 2 == 0 else (6, 7)
                        for jl in range(6):
                            for seg in range(2):
                                R.emit("tensor", "matmul", pb[yb[seg]][:],
                                       a_fm[:, jl * 1024 + tl * 128: jl * 1024 + (tl + 1) * 128],
                                       w2buf[:, jl * 1024 + seg * 512: jl * 1024 + (seg + 1) * 512],
                                       start=(jl == 0), stop=(jl == 5),
                                       waits=[("w2", w2_now),
                                              ("v", v_a_tt[0] if tl < 4 else v_a_tt[1]),
                                              y_bank_free[yb[seg]]],
                                       inc=(R.inc("t") if (jl == 5 and seg == 1) else None))
                        t_y = R.cnt["t"]
                        t_y_last = t_y
                        acl = acc_ap(tl)
                        for seg in range(2):
                            R.emit("vector", "scalar_tensor_tensor",
                                   acl[:, seg * 512:(seg + 1) * 512],
                                   pb[yb[seg]][:],
                                   comb[:, t * 8 + ei: t * 8 + ei + 1],
                                   acl[:, seg * 512:(seg + 1) * 512],
                                   op0=ALU.mult, op1=ALU.add,
                                   waits=[("t", t_y), ("v", v_comb),
                                          ("v", acc_last[tl])],
                                   inc=R.inc("v"))
                            y_bank_free[yb[seg]] = ("v", R.cnt["v"])
                        acc_last[tl] = R.cnt["v"]
                    w2_free[0] = ("t", t_y_last)
                    if pi + 1 < len(passes):
                        w2_dma(passes[pi + 1][1], passes[pi + 1][2])
            for tl in range(8):
                t = th * 8 + tl
                stsem = "st0" if tl % 2 == 0 else "st1"
                steng = "gpsimd" if tl % 2 == 0 else "sync"
                stw = [("v", acc_last[tl])]
                if R.cnt[stsem]:
                    stw.append((stsem, R.cnt[stsem]))
                R.emit(steng, "dma_start", out=d_out[t * 128:(t + 1) * 128, :],
                       in_=acc_ap(tl), waits=stw,
                       inc=R.inc(stsem))
                n_stores += 1
                acc_free[tl] = [(stsem, R.cnt[stsem])]

        return _finish(nc, R, ctx, taps, tap_outs, locals())


def _finish(nc, R, ctx, taps, tap_outs, env):
    if taps:
        for name in taps:
            buf = env[name]
            shape = list(buf.shape)
            dt = buf.dtype
            d = nc.dram_tensor(f"tap_{name}", shape, dt, kind="ExternalOutput").ap()
            R.emit("sync", "dma_start", out=d, in_=buf[:],
                   waits=[("t", R.cnt["t"]), ("v", R.cnt["v"]),
                          ("s", R.cnt["s"])] +
                         [(k, R.cnt[k]) for k in
                          ("st0", "st1", "ac0", "ac1", "ac2", "ac3")],
                   inc=R.inc("cst"))

    out_sems = {k: R.cnt[k] for k in ("st0", "st1", "ac0", "ac1", "ac2", "ac3")
                if R.cnt[k] > 0}
    cst_total = R.cnt["cst"]

    with nc.allow_low_precision("bf16 softmax-normalizer broadcast"), \
         nc.Block() as block:
        @block.tensor
        def _(E):
            for f in R.ops["tensor"]:
                f(E)

        @block.vector
        def _(E):
            for f in R.ops["vector"]:
                f(E)

        @block.scalar
        def _(E):
            for f in R.ops["scalar"]:
                f(E)

        @block.sync
        def _(E):
            for f in R.ops["sync"]:
                f(E)
            for k, v in out_sems.items():
                E.wait_ge(R.sems[k], v)
            E.wait_ge(R.sems["cst"], cst_total)

        @block.gpsimd
        def _(E):
            for f in R.ops["gpsimd"]:
                f(E)
            for k, v in out_sems.items():
                E.wait_ge(R.sems[k], v)

    ctx.close()
    return nc


def _prep_inputs(inputs):
    f32 = np.float32
    wqkvT = np.ascontiguousarray(np.asarray(inputs["L0_attn_Wqkv"], f32).T)
    woutT = np.ascontiguousarray(np.asarray(inputs["L0_attn_Wout"], f32).T)
    bout = np.asarray(inputs["L0_attn_bout"], f32).reshape(1, C)
    wl1T = np.ascontiguousarray(np.asarray(inputs["L0_attn_Wl1"], f32).T)
    bl1 = np.asarray(inputs["L0_attn_bl1"], f32).reshape(64, 1)
    wl2T = np.ascontiguousarray(np.asarray(inputs["L0_attn_Wl2"], f32).T)
    bl2 = np.asarray(inputs["L0_attn_bl2"], f32).reshape(16, 1)
    wr = np.ascontiguousarray(np.asarray(inputs["L0_ffn_Wr"], f32))
    w1 = np.ascontiguousarray(np.asarray(inputs["L0_ffn_W1"], f32))
    w2 = np.ascontiguousarray(np.asarray(inputs["L0_ffn_W2"], f32))
    w3 = np.ascontiguousarray(np.asarray(inputs["L0_ffn_W3"], f32))
    x_full = np.asarray(inputs["L0_attn_x"], f32)
    ident = np.eye(128, dtype=f32)
    import jax
    import jax.numpy as jnp
    logp_full = np.asarray(jax.jit(
        lambda: jnp.log(jnp.arange(4096, dtype=jnp.float32) + 1.0),
        backend="cpu")())

    in_maps = []
    for c in range(N_CORES):
        b = c // 2
        t0 = (c % 2) * 2048
        xs = np.ascontiguousarray(x_full[b, t0:t0 + 2048, :])
        logpT = logp_full[t0:t0 + 2048].reshape(1, T)
        in_maps.append(dict(
            x=xs, wqkvT=wqkvT, woutT=woutT, bout=bout, wl1T=wl1T, bl1=bl1,
            wl2T=wl2T, bl2=bl2, wr=wr, w1=w1, w2=w2, w3=w3, logpT=logpT,
            ident=ident))
    return in_maps, x_full.shape


def kernel(**inputs):
    in_maps, xshape = _prep_inputs(inputs)
    nc = build_program()
    res = run_bass_kernel_spmd(nc, in_maps, core_ids=list(range(N_CORES)))
    out = np.empty(xshape, np.float32)
    for c in range(N_CORES):
        b = c // 2
        t0 = (c % 2) * 2048
        out[b, t0:t0 + 2048, :] = res.results[c]["out"]
    return out
